# version 02
# baseline (speedup 1.0000x reference)
import numpy as np
import ml_dtypes
from contextlib import ExitStack
import concourse.bass as bass
import concourse.mybir as mybir
from concourse.ap import AP
from concourse.bass_utils import run_bass_kernel_spmd

F32 = mybir.dt.float32
BF16 = mybir.dt.bfloat16
AF = mybir.ActivationFunctionType
OP = mybir.AluOpType
AX = mybir.AxisListType

NCORES = 8
ROWS = 4
T = 2048
D = 1024
NB = ml_dtypes.bfloat16

_NOWRAP = {"wait_ge", "then_inc", "sem_inc", "dma_start", "engine_nop"}


class _Proxy:

    def __init__(self, c, eng, dep):
        object.__setattr__(self, '_c', c)
        object.__setattr__(self, '_e', eng)
        object.__setattr__(self, '_d', dep)

    def wait_ge(self, sem, val):
        return self._e.wait_ge(sem, val)

    def dma_start(self, *a, **k):
        return self._e.dma_start(*a, **k)

    def __getattr__(self, name):
        f = getattr(self._e, name)
        if not callable(f) or name in _NOWRAP:
            return f
        c, e, d = self._c, self._e, self._d

        def wrapped(*a, **k):
            n = c.cnt.get(d, 0)
            if n > 0:
                e.wait_ge(c.S(d), n)
            ins = f(*a, **k)
            try:
                ins.then_inc(c.S(d), 1)
                c.cnt[d] = n + 1
            except Exception:
                pass
            return ins
        return wrapped


class _BlockWrap:
    def __init__(self, c, blk):
        self._c = c
        self._b = blk

    def _mk(self, name, dep):
        dec = getattr(self._b, name)
        c = self._c

        def deco(fn):
            def body(eng):
                return fn(_Proxy(c, eng, dep) if dep else eng)
            body.__name__ = fn.__name__
            return dec(body)
        return deco

    @property
    def vector(self):
        return self._mk('vector', 'dv')

    @property
    def scalar(self):
        return self._mk('scalar', 'ds')

    @property
    def tensor(self):
        return self._mk('tensor', None)

    @property
    def sync(self):
        return self._mk('sync', None)

    @property
    def gpsimd(self):
        return self._mk('gpsimd', None)


class Ctx:
    def __init__(self):
        self.nc = bass.Bass("TRN2", target_bir_lowering=False)
        self.es = ExitStack()
        self.sem = {}
        self.cnt = {}
        self.block = None

    def start_block(self):
        self.block = _BlockWrap(self, self.es.enter_context(self.nc.Block()))
        return self.block

    def S(self, name):
        if name not in self.sem:
            self.sem[name] = self.es.enter_context(self.nc.semaphore(name))
            self.cnt[name] = 0
        return self.sem[name]

    def inc(self, ins, name, n=1):
        ins.then_inc(self.S(name), n)
        self.cnt[name] += n
        return self.cnt[name]

    def wait(self, eng, name, val=None):
        if val is None:
            val = self.cnt.get(name, 0)
        if val > 0:
            eng.wait_ge(self.S(name), val)

    def dma(self, eng, out, in_, sem='ld', **kw):
        ins = eng.dma_start(out=out, in_=in_, **kw)
        return self.inc(ins, sem, 16)

    def sb(self, name, shape, dtype):
        return self.es.enter_context(self.nc.sbuf_tensor(name, shape, dtype))

    def ps(self, name, shape=(128, 512), dtype=F32):
        return self.es.enter_context(self.nc.psum_tensor(name, list(shape), dtype))

    def close(self):
        self.es.close()
        return self.nc


def _bf(a):
    return np.ascontiguousarray(np.asarray(a).astype(NB))


def host_rope_tables_l0():
    hd = 64
    inv_freq = (1.0 / 10000.0 ** (np.arange(0, hd, 2, dtype=np.float32) / hd)).astype(np.float32)
    t = np.arange(T, dtype=np.float32)
    freqs = t[:, None] * inv_freq[None, :]
    emb = np.concatenate([np.sin(freqs), np.cos(freqs)], axis=-1).astype(np.float32)
    CT = np.zeros((128, T), np.float32)
    ST = np.zeros((128, T), np.float32)
    for p in range(128):
        j = (p % 64) // 2
        CT[p] = emb[:, 2 * j + 1]
        ST[p] = -emb[:, 2 * j] if (p % 2 == 0) else emb[:, 2 * j]
    return CT, ST


def host_pairswap():
    P = np.zeros((128, 128), np.float32)
    for i in range(128):
        P[i ^ 1, i] = 1.0
    return P


def emit_ln128(c, blk, x_ap, y_ap, cols, ci, sqscr, eps=1e-5):
    m = cols[:, 6 * ci + 0:6 * ci + 1]
    q = cols[:, 6 * ci + 1:6 * ci + 2]
    r = cols[:, 6 * ci + 2:6 * ci + 3]
    s = cols[:, 6 * ci + 3:6 * ci + 4]

    @blk.vector
    def _(v):
        v.reduce_sum(m, x_ap, AX.X)
        v.tensor_scalar_mul(m, m, 1.0 / 1024.0)

    @blk.scalar
    def _(sc):
        c.wait(sc, 'dv')
        sc.activation(sqscr[:], x_ap, AF.Square)

    @blk.vector
    def _(v):
        c.wait(v, 'ds')
        v.reduce_sum(q, sqscr[:], AX.X)
        v.tensor_scalar_mul(q, q, 1.0 / 1024.0)
        v.tensor_mul(s, m, m)
        v.tensor_sub(q, q, s)
        v.tensor_scalar_add(q, q, float(eps))

    @blk.scalar
    def _(sc):
        c.wait(sc, 'dv')
        sc.activation(s, q, AF.Sqrt)

    @blk.vector
    def _(v):
        c.wait(v, 'ds')
        v.reciprocal(r, s)
        v.tensor_scalar(y_ap, x_ap, m, r, OP.subtract, OP.mult)


def build_l0attn():
    c = Ctx()
    nc = c.nc
    x = nc.dram_tensor("x", [ROWS, T, D], F32, kind="ExternalInput").ap()
    wqkvT = nc.dram_tensor("wqkvT", [D, 3 * D], F32, kind="ExternalInput").ap()
    qkvbc = nc.dram_tensor("qkvbc", [128, 16], F32, kind="ExternalInput").ap()
    qkvbv = nc.dram_tensor("qkvbv", [1, D], F32, kind="ExternalInput").ap()
    woutT = nc.dram_tensor("woutT", [D, D], F32, kind="ExternalInput").ap()
    outb = nc.dram_tensor("outb", [1, D], F32, kind="ExternalInput").ap()
    ctab = nc.dram_tensor("ctab", [128, T], F32, kind="ExternalInput").ap()
    stab = nc.dram_tensor("stab", [128, T], F32, kind="ExternalInput").ap()
    perm = nc.dram_tensor("perm", [128, 128], F32, kind="ExternalInput").ap()
    ident = nc.dram_tensor("ident", [128, 128], F32, kind="ExternalInput").ap()
    x1 = nc.dram_tensor("x1", [ROWS, T, D], F32, kind="ExternalOutput").ap()

    xch = c.sb("xch", [128, 2, 1024], F32)
    ytm = c.sb("ytm", [128, 2, 1024], BF16)
    sqscr = c.sb("sqscr", [128, 1024], BF16)
    ybuf = c.sb("ybuf", [128, 8, 2048], BF16)
    qb = c.sb("qb", [128, 8, 2048], BF16)
    kb = c.sb("kb", [128, 8, 2048], BF16)
    vt = c.sb("vt", [128, 16, 1040], BF16)
    esx = c.sb("esx", [128, 2, 16, 256], BF16)
    wbuf = c.sb("wbuf", [128, 2, 8, 128], BF16)
    vw = c.sb("vw", [128, 8, 512], BF16)
    wob = c.sb("wob", [128, 8, 1024], BF16)
    ctb = c.sb("ctb", [128, 2048], BF16)
    stb = c.sb("stb", [128, 2048], BF16)
    permb = c.sb("permb", [128, 128], BF16)
    identb = c.sb("identb", [128, 128], BF16)
    bcols = c.sb("bcols", [128, 16], F32)
    brow = c.sb("brow", [1, D], BF16)
    obrow = c.sb("obrow", [1, D], BF16)
    lncols = c.sb("lncols", [128, 24], F32)
    rtmp = c.sb("rtmp", [128, 2, 512], BF16)
    selA = c.sb("selA", [1, 128], F32)
    selB = c.sb("selB", [1, 128], F32)
    recA = c.sb("recA", [1, 256], F32)
    recB = c.sb("recB", [1, 256], F32)
    recbc = c.sb("recbc", [128, 256], F32)
    onesr = c.sb("onesr", [1, 128], BF16)
    xres = xch.ap().rearrange("p a b -> p (a b)")[:, 0:1024]
    x1out = xch.ap().rearrange("p a b -> p (a b)")[:, 1024:2048]

    p = [c.ps(f"p{i}") for i in range(6)]
    ptA = c.ps("ptA", (128, 128), BF16)
    ptB = c.ps("ptB", (128, 128), BF16)

    blk = c.start_block()

    @blk.gpsimd
    def _(g):
        c.dma(g, ctb[:], ctab, 'gld')
        c.dma(g, stb[:], stab, 'gld')
        c.dma(g, permb[:], perm, 'gld')
        c.dma(g, identb[:], ident, 'gld')
        c.dma(g, wob[:], woutT.rearrange("(dc p) f -> p dc f", p=128), 'gld')
        c.dma(g, bcols[:], qkvbc, 'gld')
        c.dma(g, brow[:], qkvbv, 'gld')
        c.dma(g, obrow[:], outb, 'gld')

    @blk.vector
    def _(v):
        c.wait(v, 'gld')
        v.memset(selA[:, 0:64], 1.0)
        v.memset(selA[:, 64:128], 0.0)
        v.memset(selB[:, 0:64], 0.0)
        v.memset(selB[:, 64:128], 1.0)
        v.memset(onesr[:], 1.0)

    for t in range(ROWS):
        _emit_l0attn_tile(c, blk, t, x, x1, wqkvT, locals())

    @blk.sync
    def _(s):
        c.wait(s, 'st')

    return c.close()


def _emit_l0attn_tile(c, blk, t, x, x1, wqkvT, L):
    xch, ytm, sqscr, ybuf, qb, kb, vt = L['xch'], L['ytm'], L['sqscr'], L['ybuf'], L['qb'], L['kb'], L['vt']
    esx, wbuf, vw, wob, ctb, stb = L['esx'], L['wbuf'], L['vw'], L['wob'], L['ctb'], L['stb']
    permb, identb, bcols, brow, obrow = L['permb'], L['identb'], L['bcols'], L['brow'], L['obrow']
    lncols, rtmp, selA, selB, recA, recB = L['lncols'], L['rtmp'], L['selA'], L['selB'], L['recA'], L['recB']
    recbc, onesr, xres, x1out = L['recbc'], L['onesr'], L['xres'], L['x1out']
    p, ptA, ptB = L['p'], L['ptA'], L['ptB']

    for tc in range(8):
        @blk.sync
        def _(s, tc=tc):
            c.wait(s, 'mm')
            c.wait(s, 'st', c.cnt.get('x1st', 0))
            for i in range(2):
                c.dma(s, xch[:, i, :], x[t, tc * 256 + i * 128: tc * 256 + (i + 1) * 128, :], 'ld')

        @blk.vector
        def _(v, tc=tc):
            c.wait(v, 'ld')
            c.wait(v, 'mm')

        @blk.scalar
        def _(sc, tc=tc):
            c.wait(sc, 'ld')

        for i in range(2):
            emit_ln128(c, blk, xch[:, i, :], ytm[:, i, :], lncols, i, sqscr)

        first = [True]
        for i in range(2):
            for fcc in range(8):
                bank = (i * 8 + fcc) % 2

                @blk.tensor
                def _(te, tc=tc, i=i, fcc=fcc, bank=bank):
                    if first[0]:
                        c.wait(te, 'dv')
                        first[0] = False
                    c.wait(te, 'ds', c.cnt.get('tpf%d' % bank, 0))
                    ins = te.transpose((ptA if bank == 0 else ptB)[:],
                                       ytm[:, i, fcc * 128:(fcc + 1) * 128], identb[:])
                    c.inc(ins, 'mm', 1)
                    c.cnt['tpmm%d' % bank] = c.cnt['mm']

                @blk.scalar
                def _(sc, tc=tc, i=i, fcc=fcc, bank=bank):
                    c.wait(sc, 'mm', c.cnt['tpmm%d' % bank])
                    sc.copy(ybuf[:, fcc, tc * 256 + i * 128: tc * 256 + (i + 1) * 128],
                            (ptA if bank == 0 else ptB)[:])
                    c.cnt['tpf%d' % bank] = c.cnt['ds']

    @blk.tensor
    def _(te):
        c.wait(te, 'ds')

    for oc in range(16):
        ws = oc % 2

        @blk.gpsimd
        def _(g, oc=oc, ws=ws):
            c.wait(g, 'mm', c.cnt.get('wmm%d' % ws, 0))
            c.dma(g, wbuf[:, ws, :, :],
                  wqkvT[:, oc * 128:(oc + 1) * 128].rearrange("(dc p) f -> p dc f", p=128),
                  'wld%d' % ws)

        for tcc in range(4):
            bank = 2 + (oc * 4 + tcc) % 2

            @blk.tensor
            def _(te, oc=oc, ws=ws, tcc=tcc, bank=bank):
                if tcc == 0:
                    c.wait(te, 'wld%d' % ws)
                c.wait(te, 'ds', c.cnt.get('qkf%d' % bank, 0))
                for dc in range(8):
                    ins = te.matmul(p[bank][:], wbuf[:, ws, dc, :],
                                    ybuf[:, dc, tcc * 512:(tcc + 1) * 512],
                                    start=(dc == 0), stop=(dc == 7))
                c.inc(ins, 'mm', 1)
                c.cnt['qkmm%d' % bank] = c.cnt['mm']
                if tcc == 3:
                    c.cnt['wmm%d' % ws] = c.cnt['mm']

            @blk.scalar
            def _(sc, oc=oc, tcc=tcc, bank=bank):
                dest = qb if oc < 8 else kb
                hc = oc % 8
                c.wait(sc, 'mm', c.cnt['qkmm%d' % bank])
                sc.add(dest[:, hc, tcc * 512:(tcc + 1) * 512], p[bank][:],
                       bcols[:, oc:oc + 1])
                c.cnt['qkf%d' % bank] = c.cnt['ds']

    @blk.vector
    def _(v):
        c.wait(v, 'mm')
        for tk in range(16):
            ap = AP(tensor=vt.ap().tensor, offset=tk * 1040 + 64,
                    ap=[[vt.ap().ap[0][0], 128], [65, 16]])
            v.memset(ap, 1.0)
    vones_dv = c.cnt['dv']

    for hh in range(2):
        @blk.gpsimd
        def _(g, hh=hh):
            c.wait(g, 'mm', c.cnt.get('vwfree', 0))
            c.dma(g, vw[:], wqkvT[:, 2048 + hh * 512: 2048 + (hh + 1) * 512]
                  .rearrange("(dc p) f -> p dc f", p=128), 'vwld')

        for tk in range(16):
            @blk.tensor
            def _(te, tk=tk, hh=hh, vones_dv=vones_dv):
                if tk == 0:
                    c.wait(te, 'vwld')
                    c.wait(te, 'dv', vones_dv)
                bank = 4 + tk % 2
                c.wait(te, 'dv', c.cnt.get('vf%d' % bank, 0))
                for dc in range(8):
                    te.matmul(p[bank][:], ybuf[:, dc, tk * 128:(tk + 1) * 128],
                              vw[:, dc, :], start=(dc == 0), stop=False)
                ins = te.matmul(p[bank][:], onesr[:],
                                brow[:, hh * 512:(hh + 1) * 512],
                                start=False, stop=True)
                c.inc(ins, 'mm', 1)
                c.cnt['vmm%d' % bank] = c.cnt['mm']
                if tk == 15:
                    c.cnt['vwfree'] = c.cnt['mm']

            @blk.vector
            def _(v, tk=tk, hh=hh):
                bank = 4 + tk % 2
                c.wait(v, 'mm', c.cnt['vmm%d' % bank])
                dst = AP(tensor=vt.ap().tensor, offset=tk * 1040 + hh * 8 * 65,
                         ap=[[vt.ap().ap[0][0], 128], [65, 8], [1, 64]])
                v.tensor_copy(dst, p[bank][:].rearrange("q (h e) -> q h e", e=64))
                c.cnt['vf%d' % bank] = c.cnt['dv']

    first_rope = [True]
    for buf in (qb, kb):
        for hc in range(8):
            for ch in range(4):
                bank = ch % 2

                @blk.tensor
                def _(te, buf=buf, hc=hc, ch=ch, bank=bank):
                    if first_rope[0]:
                        c.wait(te, 'ds')
                        first_rope[0] = False
                    c.wait(te, 'dv', c.cnt.get('rpf%d' % bank, 0))
                    ins = te.matmul(p[bank][:], permb[:], buf[:, hc, ch * 512:(ch + 1) * 512],
                                    start=True, stop=True)
                    c.inc(ins, 'mm', 1)
                    c.cnt['rpmm%d' % bank] = c.cnt['mm']

                @blk.scalar
                def _(sc, bank=bank):
                    c.wait(sc, 'mm', c.cnt['rpmm%d' % bank])
                    c.wait(sc, 'dv', c.cnt.get('rpf%d' % bank, 0))
                    sc.copy(rtmp[:, bank, :], p[bank][:])
                    c.cnt['rpsq%d' % bank] = c.cnt['ds']

                @blk.vector
                def _(v, buf=buf, hc=hc, ch=ch, bank=bank):
                    sl = slice(ch * 512, (ch + 1) * 512)
                    c.wait(v, 'ds', c.cnt['rpsq%d' % bank])
                    v.tensor_mul(rtmp[:, bank, :], rtmp[:, bank, :], stb[:, sl])
                    v.tensor_mul(sqscr[:, 0:512], buf[:, hc, sl], ctb[:, sl])
                    v.tensor_add(buf[:, hc, sl], rtmp[:, bank, :], sqscr[:, 0:512])
                    if buf is qb:
                        v.tensor_scalar_mul(buf[:, hc, sl], buf[:, hc, sl], 0.125)
                    c.cnt['rpf%d' % bank] = c.cnt['dv']

    for hp in range(8):
        for tq in range(8):
            for tk in range(16):
                bA, bB = tk % 2, 2 + tk % 2

                @blk.tensor
                def _(te, hp=hp, tq=tq, tk=tk, bA=bA, bB=bB):
                    if tk == 0:
                        c.wait(te, 'dv')
                        c.wait(te, 'ds', c.cnt.get('bcsq', 0))
                    c.wait(te, 'ds', c.cnt.get('esf%d' % bA, 0))
                    i1 = te.matmul(p[bA][:, 0:256], kb[0:64, hp, tk * 128:(tk + 1) * 128],
                                   qb[0:64, hp, tq * 256:(tq + 1) * 256],
                                   start=True, stop=True, tile_position=(0, 0))
                    i2 = te.matmul(p[bB][:, 0:256], kb[64:128, hp, tk * 128:(tk + 1) * 128],
                                   qb[64:128, hp, tq * 256:(tq + 1) * 256],
                                   start=True, stop=True, tile_position=(64, 0))
                    c.inc(i1, 'mm', 1)
                    c.inc(i2, 'mm', 1)
                    c.cnt['scmm%d' % tk] = c.cnt['mm']

                @blk.scalar
                def _(sc, tk=tk, bA=bA):
                    c.wait(sc, 'mm', c.cnt['scmm%d' % tk])
                    sc.activation(esx[:, 0, tk, :], p[bA][:, 0:256], AF.Exp)
                    sc.activation(esx[:, 1, tk, :], p[2 + bA][:, 0:256], AF.Exp)
                    c.cnt['esf%d' % bA] = c.cnt['ds']
                    if tk == 15:
                        c.cnt['essq'] = c.cnt['ds']

            @blk.tensor
            def _(te, hp=hp, tq=tq):
                c.wait(te, 'ds', c.cnt['essq'])
                c.wait(te, 'dv')
                hA, hB = 2 * hp, 2 * hp + 1
                for tk in range(16):
                    te.matmul(p[4][0:65, 0:256], vt[:, tk, hA * 65:(hA + 1) * 65],
                              esx[:, 0, tk, :], start=(tk == 0), stop=(tk == 15))
                for tk in range(16):
                    ins = te.matmul(p[5][0:65, 0:256], vt[:, tk, hB * 65:(hB + 1) * 65],
                                    esx[:, 1, tk, :], start=(tk == 0), stop=(tk == 15))
                c.inc(ins, 'mm', 1)
                c.cnt['avmm'] = c.cnt['mm']

            @blk.vector
            def _(v, hp=hp, tq=tq):
                c.wait(v, 'mm', c.cnt['avmm'])
                v.reciprocal(recA[:], p[4][64:65, 0:256])
                v.reciprocal(recB[:], p[5][64:65, 0:256])
                c.cnt['rcdv'] = c.cnt['dv']

            @blk.tensor
            def _(te, hp=hp, tq=tq):
                c.wait(te, 'dv', c.cnt['rcdv'])
                te.matmul(p[0][:, 0:256], selA[:], recA[:], start=True, stop=False)
                ins = te.matmul(p[0][:, 0:256], selB[:], recB[:], start=False, stop=True)
                c.inc(ins, 'mm', 1)
                c.cnt['bcmm'] = c.cnt['mm']

            @blk.scalar
            def _(sc, hp=hp, tq=tq):
                c.wait(sc, 'mm', c.cnt['bcmm'])
                c.wait(sc, 'dv')
                sc.copy(recbc[:], p[0][:, 0:256])
                c.cnt['bcsq'] = c.cnt['ds']

            @blk.vector
            def _(v, hp=hp, tq=tq):
                c.wait(v, 'ds', c.cnt['bcsq'])
                sl = slice(tq * 256, (tq + 1) * 256)
                v.tensor_mul(ybuf[0:64, hp, sl], p[4][0:64, 0:256], recbc[0:64, :])
                v.tensor_mul(ybuf[64:128, hp, sl], p[5][0:64, 0:256], recbc[64:128, :])

    @blk.tensor
    def _(te):
        c.wait(te, 'dv')

    for tkk in range(16):
        @blk.sync
        def _(s, tkk=tkk):
            c.wait(s, 'dv')
            c.dma(s, xres, x[t, tkk * 128:(tkk + 1) * 128, :], 'ld')

        @blk.tensor
        def _(te, tkk=tkk):
            for hh in range(2):
                bank = (tkk * 2 + hh) % 2
                c.wait(te, 'dv', c.cnt.get('prf%d' % bank, 0))
                for ac in range(8):
                    te.matmul(p[bank][:], ybuf[:, ac, tkk * 128:(tkk + 1) * 128],
                              wob[:, ac, hh * 512:(hh + 1) * 512],
                              start=(ac == 0), stop=False)
                ins = te.matmul(p[bank][:], onesr[:], obrow[:, hh * 512:(hh + 1) * 512],
                                start=False, stop=True)
                c.inc(ins, 'mm', 1)
                c.cnt['prmm%d' % bank] = c.cnt['mm']

        @blk.vector
        def _(v, tkk=tkk):
            c.wait(v, 'ld')
            c.wait(v, 'st', c.cnt.get('x1st', 0))
            for hh in range(2):
                bank = (tkk * 2 + hh) % 2
                c.wait(v, 'mm', c.cnt['prmm%d' % bank])
                v.tensor_add(x1out[:, hh * 512:(hh + 1) * 512],
                             xres[:, hh * 512:(hh + 1) * 512], p[bank][:])
                c.cnt['prf%d' % bank] = c.cnt['dv']

        @blk.sync
        def _(s, tkk=tkk):
            c.wait(s, 'dv')
            c.dma(s, x1[t, tkk * 128:(tkk + 1) * 128, :], x1out, 'st')
            c.cnt['x1st'] = c.cnt['st']


_CACHE = {}


def l0attn_maps(inputs, xfull):
    CT, ST = host_rope_tables_l0()
    base = {
        "wqkvT": np.ascontiguousarray(inputs['L0_attn_qkv_w'].T),
        "qkvbc": np.ascontiguousarray(inputs['L0_attn_qkv_b'][:2048].reshape(16, 128).T),
        "qkvbv": np.ascontiguousarray(inputs['L0_attn_qkv_b'][2048:][None, :]),
        "woutT": np.ascontiguousarray(inputs['L0_attn_out_w'].T),
        "outb": np.ascontiguousarray(inputs['L0_attn_out_b'][None, :]),
        "ctab": CT, "stab": ST,
        "perm": host_pairswap(), "ident": np.eye(128, dtype=np.float32),
    }
    in_maps = []
    for cc in range(NCORES):
        m = dict(base)
        m["x"] = np.ascontiguousarray(xfull[cc * ROWS:(cc + 1) * ROWS])
        in_maps.append(m)
    return in_maps


def run_l0attn(inputs, xfull):
    if 'l1' not in _CACHE:
        _CACHE['l1'] = build_l0attn()
    nc = _CACHE['l1']
    res = run_bass_kernel_spmd(nc, l0attn_maps(inputs, xfull), core_ids=list(range(NCORES)))
    return np.concatenate([res.results[cc]["x1"] for cc in range(NCORES)], axis=0)


def kernel(**inputs):
    x = np.asarray(inputs['L0_attn_x'], np.float32)
    x1 = run_l0attn(inputs, x)
    slabs = run_l2a(inputs, x1)
    attns = run_l2b(slabs)
    x2ps, ln2s, w8s = run_l2c(inputs, x1, attns)
    x2 = run_l2d(inputs, ln2s, w8s, x2ps)
    x3 = run_l1attn(inputs, x2)
    x4 = run_l1ffn(inputs, x3)
    return x4


def build_l1ffn():
    c = Ctx()
    nc = c.nc
    x3 = nc.dram_tensor("x3", [ROWS, T, D], F32, kind="ExternalInput").ap()
    w1T = nc.dram_tensor("w1T", [D, 4096], F32, kind="ExternalInput").ap()
    b1c = nc.dram_tensor("b1c", [128, 32], F32, kind="ExternalInput").ap()
    w2T = nc.dram_tensor("w2T", [4096, D], F32, kind="ExternalInput").ap()
    b2r = nc.dram_tensor("b2r", [1, D], F32, kind="ExternalInput").ap()
    ident = nc.dram_tensor("ident", [128, 128], F32, kind="ExternalInput").ap()
    x4 = nc.dram_tensor("x4", [ROWS, T, D], F32, kind="ExternalOutput").ap()
    w1q = nc.dram_tensor("w1q", [D, 4096], BF16, kind="Internal").ap()

    xch = c.sb("xch", [128, 2, 1024], F32)
    ytm = c.sb("ytm", [128, 2, 1024], BF16)
    sqscr = c.sb("sqscr", [128, 1024], BF16)
    ybuf = c.sb("ybuf", [128, 8, 2048], BF16)
    gbuf = c.sb("gbuf", [128, 32, 512], BF16)
    w2b = c.sb("w2b", [128, 32, 1024], BF16)
    wtile = c.sb("wtile", [128, 4096], F32)
    wqt = c.sb("wqt", [128, 4096], BF16)
    wbuf = c.sb("wbuf", [128, 2, 8, 128], BF16)
    identb = c.sb("identb", [128, 128], BF16)
    b1cols = c.sb("b1cols", [128, 32], F32)
    b2row = c.sb("b2row", [1, D], BF16)
    lncols = c.sb("lncols", [128, 24], F32)
    onescol = c.sb("onescol", [128, 1], F32)
    onesr = c.sb("onesr", [1, 128], BF16)
    onesrf = c.sb("onesrf", [1, 128], F32)
    acc = c.sb("acc", [128, 1], F32)
    thr = c.sb("thr", [128, 2], F32)
    scal = c.sb("scal", [1, 2], F32)
    ge = c.sb("ge", [128, 4096], BF16)
    le = c.sb("le", [128, 4096], BF16)
    xres = xch.ap().rearrange("p a b -> p (a b)")[:, 0:1024]
    x4out = xch.ap().rearrange("p a b -> p (a b)")[:, 1024:2048]

    p = [c.ps(f"p{i}") for i in range(6)]
    ptA = c.ps("ptA", (128, 128), BF16)
    ptB = c.ps("ptB", (128, 128), BF16)

    blk = c.start_block()

    @blk.gpsimd
    def _(g):
        c.dma(g, identb[:], ident, 'gld')
        c.dma(g, b1cols[:], b1c, 'gld')
        c.dma(g, b2row[:], b2r, 'gld')

    @blk.vector
    def _(v):
        c.wait(v, 'gld')
        v.memset(onescol[:], 1.0)
        v.memset(onesr[:], 1.0)
        v.memset(onesrf[:], 1.0)
        v.memset(acc[:], 0.0)

    def ternarize(wsrc, nchunks, wdst_store):
        ncols = wsrc.shape[1]
        nelem = float(nchunks * 128 * ncols)
        for ch in range(nchunks):
            @blk.sync
            def _(s, ch=ch):
                c.wait(s, 'dv')
                c.dma(s, wtile[:, 0:ncols], wsrc[ch * 128:(ch + 1) * 128, :], 'ld')

            @blk.vector
            def _(v, ch=ch):
                c.wait(v, 'ld')
                v.tensor_reduce(sqscr[0:128, 0:1].bitcast(F32) if False else acc[:, 0:1],
                                wtile[:, 0:ncols], AX.X, OP.add,
                                apply_absolute_value=True) if False else None
                v.tensor_reduce(thr[:, 1:2], wtile[:, 0:ncols], AX.X, OP.add,
                                apply_absolute_value=True)
                v.tensor_add(acc[:], acc[:], thr[:, 1:2])

        @blk.tensor
        def _(te):
            c.wait(te, 'dv')
            ins = te.matmul(p[0][0:1, 0:1], acc[:], onescol[:], start=True, stop=True)
            c.inc(ins, 'mm', 1)

        @blk.vector
        def _(v):
            c.wait(v, 'mm')
            v.tensor_scalar_mul(scal[:, 0:1], p[0][0:1, 0:1], 1.0 / nelem)
            v.tensor_scalar_max(scal[:, 0:1], scal[:, 0:1], 1e-8)
            v.tensor_scalar_mul(scal[:, 0:1], scal[:, 0:1], 0.5)

        @blk.tensor
        def _(te):
            c.wait(te, 'dv')
            ins = te.matmul(p[0][:, 0:1], onesrf[:], scal[0:1, 0:1], start=True, stop=True)
            c.inc(ins, 'mm', 1)

        @blk.vector
        def _(v):
            c.wait(v, 'mm')
            v.tensor_copy(thr[:, 0:1], p[0][:, 0:1])
            v.tensor_scalar_mul(thr[:, 1:2], thr[:, 0:1], -1.0)
            v.memset(acc[:], 0.0)

        for ch in range(nchunks):
            @blk.sync
            def _(s, ch=ch):
                c.wait(s, 'dv')
                c.wait(s, 'st')
                c.dma(s, wtile[:, 0:ncols], wsrc[ch * 128:(ch + 1) * 128, :], 'ld')

            @blk.vector
            def _(v, ch=ch):
                c.wait(v, 'ld')
                v.tensor_scalar(ge[:, 0:ncols], wtile[:, 0:ncols], thr[:, 0:1], None, OP.is_ge)
                v.tensor_scalar(le[:, 0:ncols], wtile[:, 0:ncols], thr[:, 1:2], None, OP.is_le)
                v.tensor_sub(wqt[:, 0:ncols], ge[:, 0:ncols], le[:, 0:ncols])

            @blk.sync
            def _(s, ch=ch):
                c.wait(s, 'dv')
                wdst_store(s, ch)

    ternarize(w1T, 8, lambda s, ch: c.dma(s, w1q[ch * 128:(ch + 1) * 128, :], wqt[:, 0:4096], 'st'))
    w1q_st = c.cnt['st']

    ternarize(w2T, 32, lambda s, ch: c.dma(s, w2b[:, ch, :], wqt[:, 0:1024], 'st'))
    w2b_st = c.cnt['st']

    for t in range(ROWS):
        for tc in range(8):
            @blk.sync
            def _(s, tc=tc, t=t):
                c.wait(s, 'mm')
                c.wait(s, 'st', c.cnt.get('x4st', 0))
                for i in range(2):
                    c.dma(s, xch[:, i, :], x3[t, tc * 256 + i * 128: tc * 256 + (i + 1) * 128, :], 'ld')

            @blk.vector
            def _(v, tc=tc):
                c.wait(v, 'ld')
                c.wait(v, 'mm')

            @blk.scalar
            def _(sc, tc=tc):
                c.wait(sc, 'ld')

            for i in range(2):
                emit_ln128(c, blk, xch[:, i, :], ytm[:, i, :], lncols, i, sqscr)

            first = [True]
            for i in range(2):
                for fcc in range(8):
                    bank = (i * 8 + fcc) % 2

                    @blk.tensor
                    def _(te, tc=tc, i=i, fcc=fcc, bank=bank):
                        if first[0]:
                            c.wait(te, 'dv')
                            first[0] = False
                        c.wait(te, 'ds', c.cnt.get('tpf%d' % bank, 0))
                        ins = te.transpose((ptA if bank == 0 else ptB)[:],
                                           ytm[:, i, fcc * 128:(fcc + 1) * 128], identb[:])
                        c.inc(ins, 'mm', 1)
                        c.cnt['tpmm%d' % bank] = c.cnt['mm']

                    @blk.scalar
                    def _(sc, tc=tc, i=i, fcc=fcc, bank=bank):
                        c.wait(sc, 'mm', c.cnt['tpmm%d' % bank])
                        sc.copy(ybuf[:, fcc, tc * 256 + i * 128: tc * 256 + (i + 1) * 128],
                                (ptA if bank == 0 else ptB)[:])
                        c.cnt['tpf%d' % bank] = c.cnt['ds']

        for tcc in range(4):
            for fc in range(32):
                ws = fc % 2

                @blk.gpsimd
                def _(g, t=t, tcc=tcc, fc=fc, ws=ws):
                    c.wait(g, 'st', w1q_st)
                    c.wait(g, 'mm', c.cnt.get('f1wmm%d' % ws, 0))
                    c.dma(g, wbuf[:, ws, :, :],
                          w1q[:, fc * 128:(fc + 1) * 128].rearrange("(dc p) f -> p dc f", p=128),
                          'wld%d' % ws)

                @blk.tensor
                def _(te, tcc=tcc, fc=fc, ws=ws):
                    if tcc == 0 and fc == 0:
                        c.wait(te, 'ds')
                    c.wait(te, 'wld%d' % ws)
                    bank = 2 + fc % 2
                    c.wait(te, 'ds', c.cnt.get('gef%d' % bank, 0))
                    for dc in range(8):
                        ins = te.matmul(p[bank][:], wbuf[:, ws, dc, :],
                                        ybuf[:, dc, tcc * 512:(tcc + 1) * 512],
                                        start=(dc == 0), stop=(dc == 7))
                    c.inc(ins, 'mm', 1)
                    c.cnt['f1mm%d' % bank] = c.cnt['mm']
                    c.cnt['f1wmm%d' % ws] = c.cnt['mm']

                @blk.scalar
                def _(sc, fc=fc):
                    bank = 2 + fc % 2
                    c.wait(sc, 'mm', c.cnt['f1mm%d' % bank])
                    sc.activation(gbuf[:, fc, :], p[bank][:], AF.Gelu,
                                  bias=b1cols[:, fc:fc + 1])
                    c.cnt['gef%d' % bank] = c.cnt['ds']

            for tk4 in range(4):
                tkk = tcc * 4 + tk4

                @blk.sync
                def _(s, t=t, tkk=tkk):
                    c.wait(s, 'dv')
                    c.dma(s, xres, x3[t, tkk * 128:(tkk + 1) * 128, :], 'ld')

                @blk.tensor
                def _(te, tcc=tcc, tk4=tk4):
                    if tk4 == 0:
                        c.wait(te, 'ds')
                        c.wait(te, 'st', w2b_st)
                    for hh in range(2):
                        bank = 4 + hh
                        c.wait(te, 'dv', c.cnt.get('f2f%d' % bank, 0))
                        for fc in range(32):
                            te.matmul(p[bank][:], gbuf[:, fc, tk4 * 128:(tk4 + 1) * 128],
                                      w2b[:, fc, hh * 512:(hh + 1) * 512],
                                      start=(fc == 0), stop=False)
                        ins = te.matmul(p[bank][:], onesr[:], b2row[:, hh * 512:(hh + 1) * 512],
                                        start=False, stop=True)
                        c.inc(ins, 'mm', 1)
                        c.cnt['f2mm%d' % bank] = c.cnt['mm']

                @blk.vector
                def _(v, tk4=tk4):
                    c.wait(v, 'ld')
                    c.wait(v, 'st', c.cnt.get('x4st', 0))
                    for hh in range(2):
                        bank = 4 + hh
                        c.wait(v, 'mm', c.cnt['f2mm%d' % bank])
                        v.tensor_add(x4out[:, hh * 512:(hh + 1) * 512],
                                     xres[:, hh * 512:(hh + 1) * 512], p[bank][:])
                        c.cnt['f2f%d' % bank] = c.cnt['dv']

                @blk.sync
                def _(s, t=t, tkk=tkk):
                    c.wait(s, 'dv')
                    c.dma(s, x4[t, tkk * 128:(tkk + 1) * 128, :], x4out, 'st')
                    c.cnt['x4st'] = c.cnt['st']

    @blk.sync
    def _(s):
        c.wait(s, 'st')

    return c.close()


def run_l1ffn(inputs, x3full):
    if 'l4' not in _CACHE:
        _CACHE['l4'] = build_l1ffn()
    nc = _CACHE['l4']
    base = {
        "w1T": np.ascontiguousarray(inputs['L1_ffn_W1'].T),
        "b1c": np.ascontiguousarray(inputs['L1_ffn_b1'].reshape(32, 128).T),
        "w2T": np.ascontiguousarray(inputs['L1_ffn_W2'].T),
        "b2r": np.ascontiguousarray(inputs['L1_ffn_b2'][None, :]),
        "ident": np.eye(128, dtype=np.float32),
    }
    in_maps = []
    for cc in range(NCORES):
        m = dict(base)
        m["x3"] = np.ascontiguousarray(x3full[cc * ROWS:(cc + 1) * ROWS])
        in_maps.append(m)
    res = run_bass_kernel_spmd(nc, in_maps, core_ids=list(range(NCORES)))
    return np.concatenate([res.results[cc]["x4"] for cc in range(NCORES)], axis=0)


def host_rope_tables_l1():
    half = 64
    inv_freq = (10000.0 ** (np.arange(half, dtype=np.float32) / half)).astype(np.float32)
    pos = np.arange(T, dtype=np.float32)
    sinus = (pos[:, None] * inv_freq[None, :]).astype(np.float32)
    sin = np.sin(sinus).astype(np.float32)
    cos = np.cos(sinus).astype(np.float32)
    C2 = np.zeros((128, T), np.float32)
    S2 = np.zeros((128, T), np.float32)
    for p2 in range(128):
        j = p2 % 64
        C2[p2] = cos[:, j]
        S2[p2] = -sin[:, j] if p2 < 64 else sin[:, j]
    return C2, S2


def host_rothalf():
    P = np.zeros((128, 128), np.float32)
    for i in range(128):
        P[(i + 64) % 128, i] = 1.0
    return P


def build_l1attn():
    c = Ctx()
    nc = c.nc
    x2 = nc.dram_tensor("x2", [ROWS, T, D], F32, kind="ExternalInput").ap()
    uvwT = nc.dram_tensor("uvwT", [D, 4224], F32, kind="ExternalInput").ap()
    gcol = nc.dram_tensor("gcol", [128, 1], F32, kind="ExternalInput").ap()
    gamc = nc.dram_tensor("gamc", [128, 2], F32, kind="ExternalInput").ap()
    betc = nc.dram_tensor("betc", [128, 2], F32, kind="ExternalInput").ap()
    toep = nc.dram_tensor("toep", [T, T], F32, kind="ExternalInput").ap()
    owT = nc.dram_tensor("owT", [2048, D], F32, kind="ExternalInput").ap()
    rsbc = nc.dram_tensor("rsbc", [128, D], F32, kind="ExternalInput").ap()
    c2t = nc.dram_tensor("c2t", [128, T], F32, kind="ExternalInput").ap()
    s2t = nc.dram_tensor("s2t", [128, T], F32, kind="ExternalInput").ap()
    perm2 = nc.dram_tensor("perm2", [128, 128], F32, kind="ExternalInput").ap()
    ident = nc.dram_tensor("ident", [128, 128], F32, kind="ExternalInput").ap()
    x3 = nc.dram_tensor("x3", [ROWS, T, D], F32, kind="ExternalOutput").ap()
    ystash = nc.dram_tensor("ystash", [T, D], F32, kind="Internal").ap()
    ustash = nc.dram_tensor("ustash", [2048, T], BF16, kind="Internal").ap()

    xch = c.sb("xch", [128, 2, 1024], F32)
    ytile = c.sb("ytile", [128, 2, 1024], F32)
    xntm = c.sb("xntm", [128, 2, 1024], BF16)
    sqscr = c.sb("sqscr", [128, 1024], BF16)
    ybuf = c.sb("ybuf", [128, 8, 2048], BF16)
    vbuf = c.sb("vbuf", [128, 16, 2048], BF16)
    baseb = c.sb("baseb", [128, 2048], BF16)
    qf = c.sb("qf", [128, 2048], BF16)
    kf = c.sb("kf", [128, 2048], BF16)
    toept = c.sb("toept", [128, 512], F32)
    qkbt = c.sb("qkbt", [128, 512], F32)
    owb = c.sb("owb", [128, 16, 1024], BF16)
    kernT = ybuf.ap()[:, 0:4, :].rearrange("p a (b e) -> p (a b) e", e=512)
    ures = ybuf.ap()[:, 4:8, :].rearrange("p a (b e) -> p (a b) e", e=512)
    wbuf = c.sb("wbuf", [128, 2, 8, 128], BF16)
    vwt = c.sb("vwt", [128, 8, 512], BF16)
    c2b = c.sb("c2b", [128, 2048], BF16)
    s2b = c.sb("s2b", [128, 2048], BF16)
    perm2b = c.sb("perm2b", [128, 128], BF16)
    identb = c.sb("identb", [128, 128], BF16)
    gcolb = c.sb("gcolb", [128, 1], F32)
    gamb = c.sb("gamb", [128, 2], F32)
    betb = c.sb("betb", [128, 2], F32)
    rsb = c.sb("rsb", [128, 1024], F32)
    lncols = c.sb("lncols", [128, 24], F32)
    rtmp = c.sb("rtmp", [128, 2, 512], BF16)
    xres = xch.ap().rearrange("p a b -> p (a b)")[:, 0:1024]
    yres = ytile.ap().rearrange("p a b -> p (a b)")[:, 0:1024]
    x3out = ytile.ap().rearrange("p a b -> p (a b)")[:, 1024:2048]

    p = [c.ps(f"p{i}") for i in range(6)]
    ptA = c.ps("ptA", (128, 128), BF16)
    ptB = c.ps("ptB", (128, 128), BF16)

    blk = c.start_block()

    @blk.gpsimd
    def _(g):
        c.dma(g, c2b[:], c2t, 'gld')
        c.dma(g, s2b[:], s2t, 'gld')
        c.dma(g, perm2b[:], perm2, 'gld')
        c.dma(g, identb[:], ident, 'gld')
        c.dma(g, gcolb[:], gcol, 'gld')
        c.dma(g, gamb[:], gamc, 'gld')
        c.dma(g, betb[:], betc, 'gld')
        c.dma(g, rsb[:], rsbc, 'gld')
        c.dma(g, owb[:], owT.rearrange("(ec p) f -> p ec f", p=128), 'gld')

    @blk.vector
    def _(v):
        c.wait(v, 'gld')
        v.memset(sqscr[:, 0:1], 0.0)

    for t in range(ROWS):
        _emit_l1attn_tile(c, blk, t, x2, x3, uvwT, toep, ystash, ustash, locals())

    @blk.sync
    def _(s):
        c.wait(s, 'st')

    return c.close()


def _emit_l1attn_tile(c, blk, t, x2, x3, uvwT, toep, ystash, ustash, L):
    xch, ytile, xntm, sqscr, ybuf = L['xch'], L['ytile'], L['xntm'], L['sqscr'], L['ybuf']
    vbuf, baseb, qf, kf, kernT = L['vbuf'], L['baseb'], L['qf'], L['kf'], L['kernT']
    toept, qkbt, ures, owb, wbuf = L['toept'], L['qkbt'], L['ures'], L['owb'], L['wbuf']
    c2b, s2b, perm2b, identb = L['c2b'], L['s2b'], L['perm2b'], L['identb']
    gcolb, gamb, betb, rsb, lncols, rtmp = L['gcolb'], L['gamb'], L['betb'], L['rsb'], L['lncols'], L['rtmp']
    xres, yres, x3out = L['xres'], L['yres'], L['x3out']
    p, ptA, ptB = L['p'], L['ptA'], L['ptB']

    for tc in range(8):
        @blk.sync
        def _(s, tc=tc, t=t):
            c.wait(s, 'mm')
            c.wait(s, 'st', c.cnt.get('x3st', 0))
            c.wait(s, 'dv')
            for i in range(2):
                c.dma(s, xch[:, i, :], x2[t, tc * 256 + i * 128: tc * 256 + (i + 1) * 128, :], 'ld')

        @blk.vector
        def _(v, tc=tc):
            c.wait(v, 'ld')
            c.wait(v, 'mm')
            c.wait(v, 'yst')

        @blk.scalar
        def _(sc, tc=tc):
            c.wait(sc, 'ld')

        for i in range(2):
            ci = i
            m = lncols[:, 6 * ci + 0:6 * ci + 1]
            q_ = lncols[:, 6 * ci + 1:6 * ci + 2]
            r_ = lncols[:, 6 * ci + 2:6 * ci + 3]
            s_ = lncols[:, 6 * ci + 3:6 * ci + 4]
            v0 = lncols[:, 6 * ci + 4:6 * ci + 5]
            al = lncols[:, 6 * ci + 5:6 * ci + 6]

            @blk.vector
            def _(v, i=i, m=m):
                v.reduce_sum(m, xch[:, i, :], AX.X)
                v.tensor_scalar_mul(m, m, 1.0 / 1024.0)

            @blk.scalar
            def _(sc, i=i):
                c.wait(sc, 'dv')
                sc.activation(sqscr[:], xch[:, i, :], AF.Square)

            @blk.vector
            def _(v, i=i, m=m, q_=q_, s_=s_, v0=v0):
                c.wait(v, 'ds')
                v.reduce_sum(q_, sqscr[:], AX.X)
                v.tensor_scalar_mul(q_, q_, 1.0 / 1024.0)
                v.tensor_mul(s_, m, m)
                v.tensor_sub(v0, q_, s_)
                v.tensor_scalar_add(q_, v0, 1e-5)

            @blk.scalar
            def _(sc, q_=q_, s_=s_, v0=v0):
                c.wait(sc, 'dv')
                sc.activation(s_, q_, AF.Sqrt)
                sc.activation(q_, v0, AF.Sqrt)

            @blk.vector
            def _(v, i=i, m=m, q_=q_, r_=r_, s_=s_, al=al):
                c.wait(v, 'ds')
                v.reciprocal(r_, s_)
                v.tensor_scalar(ytile[:, i, :], xch[:, i, :], m, r_, OP.subtract, OP.mult)
                v.tensor_mul(al, q_, r_)
                v.tensor_scalar_max(al, al, 1e-5)
                v.reciprocal(al, al)
                v.tensor_mul(al, al, gcolb[:])
                v.tensor_scalar_mul(xntm[:, i, :], ytile[:, i, :], al)

        @blk.sync
        def _(s, tc=tc):
            c.wait(s, 'dv')
            for i in range(2):
                c.dma(s, ystash[tc * 256 + i * 128: tc * 256 + (i + 1) * 128, :],
                      ytile[:, i, :], 'yst')

        first = [True]
        for i in range(2):
            for fcc in range(8):
                bank = (i * 8 + fcc) % 2

                @blk.tensor
                def _(te, tc=tc, i=i, fcc=fcc, bank=bank):
                    if first[0]:
                        c.wait(te, 'dv')
                        first[0] = False
                    c.wait(te, 'ds', c.cnt.get('tpf%d' % bank, 0))
                    ins = te.transpose((ptA if bank == 0 else ptB)[:],
                                       xntm[:, i, fcc * 128:(fcc + 1) * 128], identb[:])
                    c.inc(ins, 'mm', 1)
                    c.cnt['tpmm%d' % bank] = c.cnt['mm']

                @blk.scalar
                def _(sc, tc=tc, i=i, fcc=fcc, bank=bank):
                    c.wait(sc, 'mm', c.cnt['tpmm%d' % bank])
                    sc.copy(ybuf[:, fcc, tc * 256 + i * 128: tc * 256 + (i + 1) * 128],
                            (ptA if bank == 0 else ptB)[:])
                    c.cnt['tpf%d' % bank] = c.cnt['ds']

    @blk.tensor
    def _(te):
        c.wait(te, 'ds')

    for oc in list(range(16)) + [32]:
        ws = oc % 2

        @blk.gpsimd
        def _(g, oc=oc, ws=ws):
            c.wait(g, 'mm', c.cnt.get('wmm%d' % ws, 0))
            c.dma(g, wbuf[:, ws, :, :],
                  uvwT[:, oc * 128:(oc + 1) * 128].rearrange("(dc p) f -> p dc f", p=128),
                  'wld%d' % ws)

        for tcc in range(4):
            bank = 2 + (oc * 4 + tcc) % 2
            uslot = tcc % 2

            @blk.tensor
            def _(te, oc=oc, ws=ws, tcc=tcc, bank=bank):
                if tcc == 0:
                    c.wait(te, 'wld%d' % ws)
                c.wait(te, 'ds', c.cnt.get('uvf%d' % bank, 0))
                for dc in range(8):
                    ins = te.matmul(p[bank][:], wbuf[:, ws, dc, :],
                                    ybuf[:, dc, tcc * 512:(tcc + 1) * 512],
                                    start=(dc == 0), stop=(dc == 7))
                c.inc(ins, 'mm', 1)
                c.cnt['uvmm%d' % bank] = c.cnt['mm']
                if tcc == 3:
                    c.cnt['wmm%d' % ws] = c.cnt['mm']

            @blk.scalar
            def _(sc, oc=oc, tcc=tcc, bank=bank, uslot=uslot):
                c.wait(sc, 'mm', c.cnt['uvmm%d' % bank])
                if oc < 16:
                    c.wait(sc, 'ust%d' % uslot, c.cnt.get('ustslot%d' % uslot, 0))
                    sc.activation(rtmp[:, uslot, :], p[bank][:], AF.Silu)
                else:
                    sc.activation(baseb[:, tcc * 512:(tcc + 1) * 512], p[bank][:], AF.Silu)
                c.cnt['uvf%d' % bank] = c.cnt['ds']
                c.cnt['uev'] = c.cnt['ds']

            if oc < 16:
                @blk.sync
                def _(s, oc=oc, tcc=tcc, uslot=uslot):
                    c.wait(s, 'ds', c.cnt['uev'])
                    c.dma(s, ustash[oc * 128:(oc + 1) * 128, tcc * 512:(tcc + 1) * 512],
                          rtmp[:, uslot, :], 'ust%d' % uslot)
                    c.cnt['ustslot%d' % uslot] = c.cnt['ust%d' % uslot]

    vwt = L['vwt']
    for hh in range(4):
        @blk.gpsimd
        def _(g, hh=hh):
            c.wait(g, 'mm', c.cnt.get('vwmm', 0))
            c.dma(g, vwt[:], uvwT[:, 2048 + hh * 512: 2048 + (hh + 1) * 512]
                  .rearrange("(dc p) f -> p dc f", p=128), 'vwl')

        for tk in range(16):
            @blk.tensor
            def _(te, hh=hh, tk=tk):
                if tk == 0:
                    c.wait(te, 'vwl')
                bank = 4 + tk % 2
                c.wait(te, 'dv', c.cnt.get('vvf%d' % bank, 0))
                for dc in range(8):
                    ins = te.matmul(p[bank][:], ybuf[:, dc, tk * 128:(tk + 1) * 128],
                                    vwt[:, dc, :],
                                    start=(dc == 0), stop=(dc == 7))
                c.inc(ins, 'mm', 1)
                c.cnt['vvmm%d' % bank] = c.cnt['mm']
                if tk == 15:
                    c.cnt['vwmm'] = c.cnt['mm']

            @blk.vector
            def _(v, hh=hh, tk=tk):
                bank = 4 + tk % 2
                c.wait(v, 'mm', c.cnt['vvmm%d' % bank])
                v.tensor_copy(vbuf[:, tk, hh * 512:(hh + 1) * 512], p[bank][:])
                c.cnt['vvf%d' % bank] = c.cnt['dv']

    for tk in range(16):
        @blk.scalar
        def _(sc, tk=tk):
            c.wait(sc, 'dv')
            sc.activation(vbuf[:, tk, :], vbuf[:, tk, :], AF.Silu)

    @blk.vector
    def _(v):
        c.wait(v, 'ds')
        v.tensor_scalar(qf[:], baseb[:], gamb[:, 0:1], betb[:, 0:1], OP.mult, OP.add)
        v.tensor_scalar(kf[:], baseb[:], gamb[:, 1:2], betb[:, 1:2], OP.mult, OP.add)

    for buf in (qf, kf):
        for ch in range(4):
            bank = ch % 2

            @blk.tensor
            def _(te, buf=buf, ch=ch, bank=bank):
                c.wait(te, 'dv')
                c.wait(te, 'ds', c.cnt.get('rp2sq%d' % bank, 0))
                ins = te.matmul(p[bank][:], perm2b[:], buf[:, ch * 512:(ch + 1) * 512],
                                start=True, stop=True)
                c.inc(ins, 'mm', 1)
                c.cnt['rp2mm%d' % bank] = c.cnt['mm']

            @blk.scalar
            def _(sc, bank=bank):
                c.wait(sc, 'mm', c.cnt['rp2mm%d' % bank])
                c.wait(sc, 'dv', c.cnt.get('rp2f%d' % bank, 0))
                c.wait(sc, 'ust%d' % bank, c.cnt.get('ustslot%d' % bank, 0))
                sc.copy(rtmp[:, bank, :], p[bank][:])
                c.cnt['rp2sq%d' % bank] = c.cnt['ds']

            @blk.vector
            def _(v, buf=buf, ch=ch, bank=bank):
                sl = slice(ch * 512, (ch + 1) * 512)
                c.wait(v, 'ds', c.cnt['rp2sq%d' % bank])
                v.tensor_mul(rtmp[:, bank, :], rtmp[:, bank, :], s2b[:, sl])
                v.tensor_mul(sqscr[:, 0:512], buf[:, sl], c2b[:, sl])
                v.tensor_add(buf[:, sl], rtmp[:, bank, :], sqscr[:, 0:512])
                c.cnt['rp2f%d' % bank] = c.cnt['dv']

    for ic in range(4):
        isl = slice(ic * 512, (ic + 1) * 512)
        for jc in range(16):
            @blk.sync
            def _(s, ic=ic, jc=jc, t=t):
                c.wait(s, 'dv', c.cnt.get('toepf', 0))
                c.dma(s, toept[:], toep[jc * 128:(jc + 1) * 128, ic * 512:(ic + 1) * 512], 'tld')

            @blk.tensor
            def _(te, ic=ic, jc=jc):
                c.wait(te, 'dv')
                bank = jc % 2
                c.wait(te, 'dv', c.cnt.get('qkf%d' % bank, 0))
                ins = te.matmul(p[bank][:], kf[:, jc * 128:(jc + 1) * 128], qf[:, isl],
                                start=True, stop=True)
                c.inc(ins, 'mm', 1)
                c.cnt['qkmm%d' % bank] = c.cnt['mm']

            @blk.vector
            def _(v, ic=ic, jc=jc):
                bank = jc % 2
                c.wait(v, 'mm', c.cnt['qkmm%d' % bank])
                c.wait(v, 'tld')
                v.tensor_add(qkbt[:], p[bank][:], toept[:])
                c.cnt['toepf'] = c.cnt['dv']
                v.scalar_tensor_tensor(kernT[:, jc, :], qkbt[:], 0.0, qkbt[:], OP.max, OP.mult)
                c.cnt['qkf%d' % bank] = c.cnt['dv']

        @blk.sync
        def _(s, ic=ic):
            c.wait(s, 'ust0')
            c.wait(s, 'ust1')
            c.wait(s, 'dv', c.cnt.get('uresf', 0))
            c.wait(s, 'mm')
            for ec in range(16):
                c.dma(s, ures[:, ec, :], ustash[ec * 128:(ec + 1) * 128, isl], 'uld')

        for ec in range(16):
            bank = 4 + ec % 2

            @blk.tensor
            def _(te, ic=ic, ec=ec, bank=bank):
                if ec == 0:
                    c.wait(te, 'dv')
                c.wait(te, 'dv', c.cnt.get('avf%d' % bank, 0))
                for jc in range(16):
                    ins = te.matmul(p[bank][:], vbuf[:, jc, ec * 128:(ec + 1) * 128],
                                    kernT[:, jc, :], start=(jc == 0), stop=(jc == 15))
                c.inc(ins, 'mm', 1)
                c.cnt['avmm%d' % bank] = c.cnt['mm']

            @blk.vector
            def _(v, ic=ic, ec=ec, bank=bank):
                c.wait(v, 'mm', c.cnt['avmm%d' % bank])
                if ec == 0:
                    c.wait(v, 'uld')
                v.scalar_tensor_tensor(sqscr[:, 0:512], p[bank][:], 1.0 / 128.0,
                                       ures[:, ec, :], OP.mult, OP.mult)
                v.tensor_copy(ures[:, ec, :], sqscr[:, 0:512])
                c.cnt['avf%d' % bank] = c.cnt['dv']
                c.cnt['uresf'] = c.cnt['dv']

        for tk4 in range(4):
            tkk = ic * 4 + tk4

            @blk.sync
            def _(s, t=t, tkk=tkk):
                c.wait(s, 'dv')
                c.wait(s, 'yst')
                c.dma(s, xres, x2[t, tkk * 128:(tkk + 1) * 128, :], 'ld')
                c.dma(s, yres, ystash[tkk * 128:(tkk + 1) * 128, :], 'ld')

            @blk.tensor
            def _(te, ic=ic, tk4=tk4):
                if tk4 == 0:
                    c.wait(te, 'dv')
                for hh in range(2):
                    bank = (tk4 * 2 + hh) % 2
                    c.wait(te, 'dv', c.cnt.get('prf%d' % bank, 0))
                    for ec in range(16):
                        ins = te.matmul(p[bank][:], ures[:, ec, tk4 * 128:(tk4 + 1) * 128],
                                        owb[:, ec, hh * 512:(hh + 1) * 512],
                                        start=(ec == 0), stop=(ec == 15))
                    c.inc(ins, 'mm', 1)
                    c.cnt['prmm%d' % bank] = c.cnt['mm']

            @blk.vector
            def _(v, ic=ic, tk4=tk4):
                c.wait(v, 'ld')
                c.wait(v, 'st', c.cnt.get('x3st', 0))
                for hh in range(2):
                    bank = (tk4 * 2 + hh) % 2
                    c.wait(v, 'mm', c.cnt['prmm%d' % bank])
                    hsl = slice(hh * 512, (hh + 1) * 512)
                    v.tensor_mul(x3out[:, hsl], yres[:, hsl], rsb[:, hsl])
                    v.tensor_add(x3out[:, hsl], x3out[:, hsl], xres[:, hsl])
                    v.tensor_add(x3out[:, hsl], x3out[:, hsl], p[bank][:])
                    c.cnt['prf%d' % bank] = c.cnt['dv']

            @blk.sync
            def _(s, t=t, tkk=tkk):
                c.wait(s, 'dv')
                c.dma(s, x3[t, tkk * 128:(tkk + 1) * 128, :], x3out, 'st')
                c.cnt['x3st'] = c.cnt['st']


def run_l1attn(inputs, x2full):
    if 'l3' not in _CACHE:
        _CACHE['l3'] = build_l1attn()
    nc = _CACHE['l3']
    C2, S2 = host_rope_tables_l1()
    w_rel = inputs['L1_attn_w_rel']
    idx = np.arange(T)[None, :] - np.arange(T)[:, None] + (T - 1)
    idx = np.clip(idx, 0, len(w_rel) - 1)
    toepM = np.ascontiguousarray(w_rel[idx.T].astype(np.float32))
    base = {
        "uvwT": np.ascontiguousarray(inputs['L1_attn_uv_w'].T),
        "gcol": np.ascontiguousarray(np.repeat(inputs['L1_attn_g'].reshape(1, 1), 128, 0)),
        "gamc": np.ascontiguousarray(inputs['L1_attn_gamma'].T),
        "betc": np.ascontiguousarray(inputs['L1_attn_beta'].T),
        "toep": toepM,
        "owT": np.ascontiguousarray(inputs['L1_attn_o_w'].T),
        "rsbc": np.ascontiguousarray(np.repeat(inputs['L1_attn_res_scale'][None, :], 128, 0)),
        "c2t": C2, "s2t": S2,
        "perm2": host_rothalf(), "ident": np.eye(128, dtype=np.float32),
    }
    in_maps = []
    for cc in range(NCORES):
        m = dict(base)
        m["x2"] = np.ascontiguousarray(x2full[cc * ROWS:(cc + 1) * ROWS])
        in_maps.append(m)
    res = run_bass_kernel_spmd(nc, in_maps, core_ids=list(range(NCORES)))
    return np.concatenate([res.results[cc]["x3"] for cc in range(NCORES)], axis=0)


def build_l2a():
    c = Ctx()
    nc = c.nc
    x1 = nc.dram_tensor("x1", [ROWS, T, D], F32, kind="ExternalInput").ap()
    wq0T = nc.dram_tensor("wq0T", [D, 1536], F32, kind="ExternalInput").ap()
    ln1wc = nc.dram_tensor("ln1wc", [128, 8], F32, kind="ExternalInput").ap()
    ln1bc = nc.dram_tensor("ln1bc", [128, 8], F32, kind="ExternalInput").ap()
    ident = nc.dram_tensor("ident", [128, 128], F32, kind="ExternalInput").ap()
    slab = nc.dram_tensor("slab", [8, 1536, 4, 256], BF16, kind="ExternalOutput").ap()

    xch = c.sb("xch", [128, 2, 1024], F32)
    ytm = c.sb("ytm", [128, 2, 1024], BF16)
    sqscr = c.sb("sqscr", [128, 1024], BF16)
    ybuf = c.sb("ybuf", [128, 8, 2048], BF16)
    wq0 = c.sb("wq0", [128, 8, 1536], BF16)
    identb = c.sb("identb", [128, 128], BF16)
    w1c = c.sb("w1c", [128, 8], F32)
    b1c = c.sb("b1c", [128, 8], BF16)
    bpc = c.sb("bpc", [128, 12], F32)
    lncols = c.sb("lncols", [128, 24], F32)
    sev = c.sb("sev", [128, 512], BF16)

    p = [c.ps(f"p{i}") for i in range(4)]
    ptA = c.ps("ptA", (128, 128), BF16)
    ptB = c.ps("ptB", (128, 128), BF16)

    blk = c.start_block()

    @blk.gpsimd
    def _(g):
        c.dma(g, identb[:], ident, 'gld')
        c.dma(g, w1c[:], ln1wc, 'gld')
        c.dma(g, b1c[:], ln1bc, 'gld')
        c.dma(g, wq0[:], wq0T.rearrange("(dc p) f -> p dc f", p=128), 'gld')

    for fc in range(12):
        @blk.tensor
        def _(te, fc=fc):
            if fc == 0:
                c.wait(te, 'gld')
            else:
                c.wait(te, 'dv')
            for dc in range(8):
                ins = te.matmul(p[0][:, 0:1], wq0[:, dc, fc * 128:(fc + 1) * 128],
                                b1c[:, dc:dc + 1], start=(dc == 0), stop=(dc == 7))
            c.inc(ins, 'mm', 1)

        @blk.vector
        def _(v, fc=fc):
            c.wait(v, 'mm')
            v.tensor_copy(bpc[:, fc:fc + 1], p[0][:, 0:1])

    for dc in range(8):
        @blk.vector
        def _(v, dc=dc):
            if dc == 0:
                c.wait(v, 'mm')
            v.tensor_scalar_mul(wq0[:, dc, :], wq0[:, dc, :], w1c[:, dc:dc + 1])
    wq0_dv = c.cnt['dv']

    for t in range(ROWS):
        for tc in range(8):
            @blk.sync
            def _(s, tc=tc, t=t):
                c.wait(s, 'mm')
                for i in range(2):
                    c.dma(s, xch[:, i, :], x1[t, tc * 256 + i * 128: tc * 256 + (i + 1) * 128, :], 'ld')

            @blk.vector
            def _(v, tc=tc):
                c.wait(v, 'ld')
                c.wait(v, 'mm')

            @blk.scalar
            def _(sc, tc=tc):
                c.wait(sc, 'ld')

            for i in range(2):
                emit_ln128(c, blk, xch[:, i, :], ytm[:, i, :], lncols, i, sqscr)

            first = [True]
            for i in range(2):
                for fcc in range(8):
                    bank = (i * 8 + fcc) % 2

                    @blk.tensor
                    def _(te, tc=tc, i=i, fcc=fcc, bank=bank):
                        if first[0]:
                            c.wait(te, 'dv')
                            first[0] = False
                        c.wait(te, 'ds', c.cnt.get('tpf%d' % bank, 0))
                        ins = te.transpose((ptA if bank == 0 else ptB)[:],
                                           ytm[:, i, fcc * 128:(fcc + 1) * 128], identb[:])
                        c.inc(ins, 'mm', 1)
                        c.cnt['tpmm%d' % bank] = c.cnt['mm']

                    @blk.scalar
                    def _(sc, tc=tc, i=i, fcc=fcc, bank=bank):
                        c.wait(sc, 'mm', c.cnt['tpmm%d' % bank])
                        sc.copy(ybuf[:, fcc, tc * 256 + i * 128: tc * 256 + (i + 1) * 128],
                                (ptA if bank == 0 else ptB)[:])
                        c.cnt['tpf%d' % bank] = c.cnt['ds']

        for fc in range(12):
            for tcc in range(4):
                bank = 2 + (fc * 4 + tcc) % 2

                @blk.tensor
                def _(te, fc=fc, tcc=tcc, bank=bank, t=t):
                    if fc == 0 and tcc == 0:
                        c.wait(te, 'ds')
                        c.wait(te, 'dv', wq0_dv)
                    c.wait(te, 'ds', c.cnt.get('slf%d' % bank, 0))
                    for dc in range(8):
                        ins = te.matmul(p[bank][:], wq0[:, dc, fc * 128:(fc + 1) * 128],
                                        ybuf[:, dc, tcc * 512:(tcc + 1) * 512],
                                        start=(dc == 0), stop=(dc == 7))
                    c.inc(ins, 'mm', 1)
                    c.cnt['slmm%d' % bank] = c.cnt['mm']

                @blk.scalar
                def _(sc, fc=fc, tcc=tcc, bank=bank):
                    c.wait(sc, 'mm', c.cnt['slmm%d' % bank])
                    c.wait(sc, 'sst', c.cnt.get('sevst', 0))
                    sc.add(sev[:], p[bank][:], bpc[:, fc:fc + 1])
                    c.cnt['slf%d' % bank] = c.cnt['ds']
                    c.cnt['sevds'] = c.cnt['ds']

                @blk.sync
                def _(s, fc=fc, tcc=tcc, t=t):
                    c.wait(s, 'ds', c.cnt['sevds'])
                    for sh in range(2):
                        shard = tcc * 2 + sh
                        dst = AP(tensor=slab.tensor,
                                 offset=shard * (1536 * 4 * 256) + fc * 128 * 4 * 256 + t * 256,
                                 ap=[[4 * 256, 128], [1, 256]])
                        c.dma(s, dst, sev[:, sh * 256:(sh + 1) * 256], 'sst')
                    c.cnt['sevst'] = c.cnt['sst']

    @blk.sync
    def _(s):
        c.wait(s, 'sst')

    return c.close()


def run_l2a(inputs, x1full):
    if 'l2a' not in _CACHE:
        _CACHE['l2a'] = build_l2a()
    nc = _CACHE['l2a']
    base = {
        "wq0T": np.ascontiguousarray(inputs['L0_ffn_qkv_weight'].T),
        "ln1wc": np.ascontiguousarray(inputs['L0_ffn_ln1_weight'].reshape(8, 128).T),
        "ln1bc": np.ascontiguousarray(inputs['L0_ffn_ln1_bias'].reshape(8, 128).T),
        "ident": np.eye(128, dtype=np.float32),
    }
    in_maps = []
    for cc in range(NCORES):
        m = dict(base)
        m["x1"] = np.ascontiguousarray(x1full[cc * ROWS:(cc + 1) * ROWS])
        in_maps.append(m)
    res = run_bass_kernel_spmd(nc, in_maps, core_ids=list(range(NCORES)))
    return [res.results[cc]["slab"] for cc in range(NCORES)]


def build_l2b(nkh=4):
    c = Ctx()
    nc = c.nc
    slab = nc.dram_tensor("slab", [1536, 256, 32], BF16, kind="ExternalInput").ap()
    maskc = nc.dram_tensor("maskc", [128, 32], F32, kind="ExternalInput").ap()
    ident2 = nc.dram_tensor("ident2", [128, 64], F32, kind="ExternalInput").ap()
    aout = nc.dram_tensor("aout", [1024, 256, 32], BF16, kind="ExternalOutput").ap()

    qkh = c.sb("qkh", [128, 2, 8192], BF16)
    qodd = c.sb("qodd", [128, 2, 8192], BF16)
    kvb = c.sb("kvb", [128, 8192], BF16)
    vtm3 = c.sb("vtm3", [32, 256, 64], BF16)
    coll = c.sb("coll", [64, 8192], BF16)
    expb = c.sb("expb", [32, 1024], BF16)
    bcbf = c.sb("bcbf", [32, 512], F32)
    recb = c.sb("recb", [1, 512], F32)
    maskb = c.sb("maskb", [128, 32], BF16)
    onescl = c.sb("onescl", [32, 1], BF16)
    onesrw = c.sb("onesrw", [1, 32], F32)
    identb = c.sb("identb", [128, 64], BF16)

    p = [c.ps(f"p{i}") for i in range(6)]
    ptv = c.ps("ptv", (32, 64), BF16)

    blkw = c.start_block()

    @blkw.gpsimd
    def _(g):
        c.dma(g, maskb[:], maskc, 'gld')
        c.dma(g, identb[:], ident2, 'gld')

    @blkw.vector
    def _(v):
        c.wait(v, 'gld')
        v.memset(onescl[:], 1.0)
        v.memset(onesrw[:], 1.0)

    for kh in range(nkh):
        @blkw.sync
        def _(s, kh=kh):
            c.wait(s, 'mm')
            c.wait(s, 'kcp')
            for qc in range(2):
                c.dma(s, qkh[:, qc, :],
                      slab[(kh * 384 + qc * 128):(kh * 384 + (qc + 1) * 128), :, :]
                      .rearrange("p a b -> p (a b)"), 'ld')
            c.dma(s, kvb[:],
                  slab[(kh * 384 + 256):(kh * 384 + 384), :, :]
                  .rearrange("p a b -> p (a b)"), 'ld')

        @blkw.sync
        def _(s, kh=kh):
            c.wait(s, 'ld')
            for j in range(2):
                c.dma(s, qodd[0:64, j, :], qkh[64:128, j, :], 'kcp')

        rs_kv = kvb.ap().ap[0][0]
        for col in range(256):
            @blkw.tensor
            def _(te, kh=kh, col=col):
                if col == 0:
                    c.wait(te, 'kcp')
                    c.wait(te, 'gld')
                c.wait(te, 'ds', c.cnt.get('vtf', 0))
                src = AP(tensor=kvb.ap().tensor, offset=64 * rs_kv + col * 32,
                         ap=[[rs_kv, 64], [1, 32]])
                ins = te.transpose(ptv[:], src, identb[64:128, 0:64])
                c.inc(ins, 'mm', 1)
                c.cnt['vtmm'] = c.cnt['mm']

            @blkw.scalar
            def _(sc, kh=kh, col=col):
                c.wait(sc, 'mm', c.cnt['vtmm'])
                sc.copy(vtm3[:, col, :], ptv[:])
                c.cnt['vtf'] = c.cnt['ds']

        rs_q = qkh.ap().ap[0][0]
        for g in range(4):
            qsrc = qkh if (g % 2 == 0) else qodd
            qslot = g // 2
            for bf in range(16):
                for sp in range(16):
                    col = bf * 16 + sp

                    @blkw.tensor
                    def _(te, kh=kh, g=g, bf=bf, sp=sp, col=col, qsrc=qsrc, qslot=qslot):
                        if sp == 0:
                            c.wait(te, 'ds', c.cnt.get('exf', 0))
                        stat = AP(tensor=kvb.ap().tensor, offset=col * 32,
                                  ap=[[rs_kv, 64], [1, 32]])
                        rhs = AP(tensor=qsrc.ap().tensor,
                                 offset=qslot * 8192 + col * 32,
                                 ap=[[rs_q, 64], [1, 32]])
                        ins = te.matmul(p[0][0:32, sp * 32:(sp + 1) * 32],
                                        stat, rhs, start=True, stop=True)
                        c.inc(ins, 'mm', 1)
                        c.cnt['scmm'] = c.cnt['mm']

                @blkw.scalar
                def _(sc, kh=kh, g=g, bf=bf):
                    c.wait(sc, 'mm', c.cnt['scmm'])
                    c.wait(sc, 'dv', c.cnt.get('nrmf', 0))
                    sc.activation(expb[:, (bf % 2) * 512:(bf % 2) * 512 + 512],
                                  p[0][0:32, :], AF.Exp, scale=0.125)
                    c.cnt['exf'] = c.cnt['ds']

                @blkw.vector
                def _(v, kh=kh, g=g, bf=bf):
                    c.wait(v, 'ds', c.cnt['exf'])
                    esl = expb.ap()[:, (bf % 2) * 512:(bf % 2) * 512 + 512]
                    mask_ap = AP(tensor=maskb.ap().tensor, offset=0,
                                 ap=[[maskb.ap().ap[0][0], 32], [0, 16], [1, 32]])
                    ev = esl.rearrange("p (a b) -> p a b", b=32)
                    v.tensor_mul(ev, ev, mask_ap)
                    c.cnt['mskf'] = c.cnt['dv']

                @blkw.tensor
                def _(te, kh=kh, g=g, bf=bf):
                    c.wait(te, 'dv', c.cnt['mskf'])
                    c.wait(te, 'dv', c.cnt.get('rcf', 0))
                    ins = te.matmul(p[4][0:1, :], onescl[:],
                                    expb[:, (bf % 2) * 512:(bf % 2) * 512 + 512],
                                    start=True, stop=True)
                    c.inc(ins, 'mm', 1)
                    c.cnt['dnmm'] = c.cnt['mm']

                @blkw.vector
                def _(v, kh=kh, g=g, bf=bf):
                    c.wait(v, 'mm', c.cnt['dnmm'])
                    v.reciprocal(recb[:], p[4][0:1, :])
                    c.cnt['rcf'] = c.cnt['dv']

                @blkw.tensor
                def _(te, kh=kh, g=g, bf=bf):
                    c.wait(te, 'dv', c.cnt['rcf'])
                    c.wait(te, 'ds', c.cnt.get('bcf', 0))
                    ins = te.matmul(p[4][0:32, :], onesrw[:], recb[:], start=True, stop=True)
                    c.inc(ins, 'mm', 1)
                    c.cnt['bcmm'] = c.cnt['mm']

                @blkw.scalar
                def _(sc, kh=kh, g=g, bf=bf):
                    c.wait(sc, 'mm', c.cnt['bcmm'])
                    sc.copy(bcbf[:], p[4][0:32, :])
                    c.cnt['bcf'] = c.cnt['ds']

                @blkw.vector
                def _(v, kh=kh, g=g, bf=bf):
                    c.wait(v, 'ds', c.cnt['bcf'])
                    esl = expb.ap()[:, (bf % 2) * 512:(bf % 2) * 512 + 512]
                    v.tensor_mul(esl, esl, bcbf[:])
                    c.cnt['nrmf'] = c.cnt['dv']

                for sp in range(16):
                    col = bf * 16 + sp

                    @blkw.tensor
                    def _(te, kh=kh, g=g, bf=bf, sp=sp, col=col):
                        if sp == 0:
                            c.wait(te, 'dv', c.cnt['nrmf'])
                            c.wait(te, 'dv', c.cnt.get('avev', 0))
                        stat = vtm3.ap()[:, col, :]
                        rhs = expb.ap()[:, (bf % 2) * 512 + sp * 32:(bf % 2) * 512 + (sp + 1) * 32]
                        ins = te.matmul(p[5][0:64, sp * 32:(sp + 1) * 32],
                                        stat, rhs, start=True, stop=True)
                        c.inc(ins, 'mm', 1)
                        if sp == 15:
                            c.cnt['avmm'] = c.cnt['mm']

                    if sp == 15:
                        @blkw.vector
                        def _(v, kh=kh, g=g, bf=bf):
                            c.wait(v, 'mm', c.cnt['avmm'])
                            v.tensor_copy(coll[:, bf * 512:(bf + 1) * 512], p[5][0:64, :])
                            c.cnt['avev'] = c.cnt['dv']

            @blkw.sync
            def _(s, kh=kh, g=g):
                c.wait(s, 'dv')
                h0 = kh * 4 + g
                dst = aout[(h0 * 64):(h0 * 64 + 64), :, :].rearrange("p a b -> p (a b)")
                c.dma(s, dst, coll[:], 'ast')
                c.cnt['collst'] = c.cnt['ast']

            @blkw.vector
            def _(v):
                c.wait(v, 'ast', c.cnt['collst'])

    @blkw.sync
    def _(s):
        c.wait(s, 'ast')

    return c.close()


def l2b_consts():
    mask = np.zeros((128, 32), np.float32)
    for pp in range(128):
        for s in range(32):
            mask[pp, s] = 1.0 if (pp % 32) <= s else 0.0
    return mask


def run_l2b(slabs):
    if 'l2b' not in _CACHE:
        _CACHE['l2b'] = build_l2b()
    nc = _CACHE['l2b']
    mask = l2b_consts()
    base = {"maskc": mask,
            "ident2": np.ascontiguousarray(np.tile(np.eye(64, dtype=np.float32), (2, 1)))}
    in_maps = []
    for cc in range(NCORES):
        m = dict(base)
        st = np.stack([slabs[r][cc] for r in range(NCORES)])
        m["slab"] = np.ascontiguousarray(st.transpose(1, 3, 0, 2).reshape(1536, 256, 32))
        in_maps.append(m)
    res = run_bass_kernel_spmd(nc, in_maps, core_ids=list(range(NCORES)))
    return [res.results[cc]["aout"] for cc in range(NCORES)]


def build_l2c():
    c = Ctx()
    nc = c.nc
    x1 = nc.dram_tensor("x1", [ROWS, T, D], F32, kind="ExternalInput").ap()
    attnfm = nc.dram_tensor("attnfm", [D, ROWS, T], BF16, kind="ExternalInput").ap()
    projT = nc.dram_tensor("projT", [D, D], F32, kind="ExternalInput").ap()
    ln2wb = nc.dram_tensor("ln2wb", [128, D], F32, kind="ExternalInput").ap()
    ln2bb = nc.dram_tensor("ln2bb", [128, D], F32, kind="ExternalInput").ap()
    rw = nc.dram_tensor("rw", [D, 8], F32, kind="ExternalInput").ap()
    ident = nc.dram_tensor("ident", [128, 128], F32, kind="ExternalInput").ap()
    x2p = nc.dram_tensor("x2p", [ROWS, T, D], F32, kind="ExternalOutput").ap()
    ln2o = nc.dram_tensor("ln2o", [ROWS * T, D], BF16, kind="ExternalOutput").ap()
    w8o = nc.dram_tensor("w8o", [ROWS * T, 8], F32, kind="ExternalOutput").ap()

    xche = c.sb("xche", [128, 1024], F32)
    afm = c.sb("afm", [128, 8, 128], BF16)
    htile = c.sb("htile", [128, 1024], F32)
    hatile = c.sb("hatile", [128, 1024], F32)
    l2tile = c.sb("l2tile", [128, 1024], BF16)
    l2fm = c.sb("l2fm", [128, 8, 128], BF16)
    outt = c.sb("outt", [128, 1024], F32)
    sqscr = c.sb("sqscr", [128, 1024], BF16)
    pwb = c.sb("pwb", [128, 8, 1024], BF16)
    rwb = c.sb("rwb", [128, 8, 8], BF16)
    w2b = c.sb("w2bc", [128, 1024], F32)
    b2b = c.sb("b2bc", [128, 1024], F32)
    identb = c.sb("identb", [128, 128], BF16)
    lncols = c.sb("lncols", [128, 24], F32)
    tcols = c.sb("tcols", [128, 12], F32)

    p = [c.ps(f"p{i}") for i in range(4)]
    ptA = c.ps("ptA", (128, 128), BF16)
    ptB = c.ps("ptB", (128, 128), BF16)

    blk = c.start_block()

    @blk.gpsimd
    def _(g):
        c.dma(g, pwb[:], projT.rearrange("(dc p) f -> p dc f", p=128), 'gld')
        c.dma(g, rwb[:], rw.rearrange("(dc p) f -> p dc f", p=128), 'gld')
        c.dma(g, w2b[:], ln2wb, 'gld')
        c.dma(g, b2b[:], ln2bb, 'gld')
        c.dma(g, identb[:], ident, 'gld')

    for t in range(ROWS):
        for tkk in range(16):
            tok0 = t * T + tkk * 128

            @blk.sync
            def _(s, t=t, tkk=tkk):
                c.wait(s, 'dv')
                c.wait(s, 'ds')
                c.dma(s, xche[:], x1[t, tkk * 128:(tkk + 1) * 128, :], 'ld')
                src = AP(tensor=attnfm.tensor, offset=t * T + tkk * 128,
                         ap=[[ROWS * T, 128], [128 * ROWS * T, 8], [1, 128]])
                c.dma(s, afm[:], src, 'ld')

            @blk.vector
            def _(v, tkk=tkk):
                c.wait(v, 'ld')

            @blk.scalar
            def _(sc, tkk=tkk):
                c.wait(sc, 'ld')

            emit_ln128(c, blk, xche[:], htile[:], lncols, 0, sqscr)

            @blk.tensor
            def _(te, tkk=tkk):
                if tkk == 0:
                    c.wait(te, 'gld')
                c.wait(te, 'ld')
                for hh in range(2):
                    bank = hh
                    c.wait(te, 'dv', c.cnt.get('prf%d' % bank, 0))
                    for ac in range(8):
                        ins = te.matmul(p[bank][:], afm[:, ac, :],
                                        pwb[:, ac, hh * 512:(hh + 1) * 512],
                                        start=(ac == 0), stop=(ac == 7))
                    c.inc(ins, 'mm', 1)
                    c.cnt['prmm%d' % bank] = c.cnt['mm']

            @blk.vector
            def _(v, t=t, tkk=tkk):
                c.wait(v, 'dv')
                c.wait(v, 'wst', c.cnt.get('wst', 0))
                for hh in range(2):
                    bank = hh
                    c.wait(v, 'mm', c.cnt['prmm%d' % bank])
                    hsl = slice(hh * 512, (hh + 1) * 512)
                    v.tensor_add(hatile[:, hsl], htile[:, hsl], p[bank][:])
                    v.tensor_add(outt[:, hsl], xche[:, hsl], hatile[:, hsl])
                    c.cnt['prf%d' % bank] = c.cnt['dv']

            @blk.sync
            def _(s, t=t, tkk=tkk):
                c.wait(s, 'dv')
                c.dma(s, x2p[t, tkk * 128:(tkk + 1) * 128, :], outt[:], 'st')

            emit_ln128(c, blk, hatile[:], sqscr[:], lncols, 1, sqscr) if False else None
            m_ = lncols[:, 6:7]
            q_ = lncols[:, 7:8]
            r_ = lncols[:, 8:9]
            s_ = lncols[:, 9:10]

            @blk.vector
            def _(v):
                v.reduce_sum(m_, hatile[:], AX.X)
                v.tensor_scalar_mul(m_, m_, 1.0 / 1024.0)

            @blk.scalar
            def _(sc):
                c.wait(sc, 'dv')
                sc.activation(sqscr[:], hatile[:], AF.Square)

            @blk.vector
            def _(v):
                c.wait(v, 'ds')
                v.reduce_sum(q_, sqscr[:], AX.X)
                v.tensor_scalar_mul(q_, q_, 1.0 / 1024.0)
                v.tensor_mul(s_, m_, m_)
                v.tensor_sub(q_, q_, s_)
                v.tensor_scalar_add(q_, q_, 1e-5)

            @blk.scalar
            def _(sc):
                c.wait(sc, 'dv')
                sc.activation(s_, q_, AF.Sqrt)

            @blk.vector
            def _(v, t=t, tkk=tkk, tok0=tok0):
                c.wait(v, 'ds')
                c.wait(v, 'st', c.cnt.get('st', 0))
                v.reciprocal(r_, s_)
                v.tensor_scalar(outt[:], hatile[:], m_, r_, OP.subtract, OP.mult)
                v.tensor_mul(outt[:], outt[:], w2b[:])
                v.tensor_add(l2tile[:], outt[:], b2b[:])

            @blk.sync
            def _(s, tok0=tok0):
                c.wait(s, 'dv')
                c.dma(s, ln2o[tok0:tok0 + 128, :], l2tile[:], 'lst')

            for fcc in range(8):
                bank = fcc % 2

                @blk.tensor
                def _(te, fcc=fcc, bank=bank):
                    if fcc == 0:
                        c.wait(te, 'dv')
                    c.wait(te, 'ds', c.cnt.get('tpf%d' % bank, 0))
                    ins = te.transpose((ptA if bank == 0 else ptB)[:],
                                       l2tile[:, fcc * 128:(fcc + 1) * 128], identb[:])
                    c.inc(ins, 'mm', 1)
                    c.cnt['tpmm%d' % bank] = c.cnt['mm']

                @blk.scalar
                def _(sc, fcc=fcc, bank=bank):
                    c.wait(sc, 'mm', c.cnt['tpmm%d' % bank])
                    sc.copy(l2fm[:, fcc, :], (ptA if bank == 0 else ptB)[:])
                    c.cnt['tpf%d' % bank] = c.cnt['ds']

            @blk.tensor
            def _(te, tkk=tkk):
                c.wait(te, 'ds')
                for dc in range(8):
                    ins = te.matmul(p[2][:, 0:8], l2fm[:, dc, :], rwb[:, dc, :],
                                    start=(dc == 0), stop=(dc == 7))
                c.inc(ins, 'mm', 1)
                c.cnt['rtmm'] = c.cnt['mm']

            @blk.vector
            def _(v, tok0=tok0):
                c.wait(v, 'mm', c.cnt['rtmm'])
                lg = tcols[:, 0:8]
                l1 = tcols[:, 8:9]
                l2_ = tcols[:, 9:10]
                w1_ = tcols[:, 10:11]
                w2_ = tcols[:, 11:12]
                v.tensor_copy(lg, p[2][:, 0:8])
                v.reduce_max(l1, lg, AX.X)
                v.tensor_scalar(sqscr[:, 0:8].bitcast(F32)[:, 0:8] if False else outt[:, 0:8],
                                lg, l1, None, OP.is_ge)
                v.scalar_tensor_tensor(outt[:, 8:16], outt[:, 0:8], -1e30, lg, OP.mult, OP.add)
                v.reduce_max(l2_, outt[:, 8:16], AX.X)
                v.tensor_sub(w1_, l1, l2_)
                c.cnt['tpv'] = c.cnt['dv']

            @blk.scalar
            def _(sc):
                c.wait(sc, 'dv', c.cnt['tpv'])
                sc.activation(tcols[:, 10:11], tcols[:, 10:11], AF.Sigmoid)

            @blk.vector
            def _(v, tok0=tok0):
                c.wait(v, 'ds')
                lg = tcols[:, 0:8]
                l1 = tcols[:, 8:9]
                l2_ = tcols[:, 9:10]
                w1_ = tcols[:, 10:11]
                w2_ = tcols[:, 11:12]
                v.tensor_scalar(w2_, w1_, -1.0, 1.0, OP.mult, OP.add)
                v.tensor_scalar(outt[:, 8:16], lg, l2_, None, OP.is_ge)
                v.tensor_sub(outt[:, 8:16], outt[:, 8:16], outt[:, 0:8])
                v.tensor_scalar(outt[:, 0:8], outt[:, 0:8], w1_, None, OP.mult)
                v.tensor_scalar(outt[:, 8:16], outt[:, 8:16], w2_, None, OP.mult)
                v.tensor_add(outt[:, 0:8], outt[:, 0:8], outt[:, 8:16])

            @blk.sync
            def _(s, tok0=tok0):
                c.wait(s, 'dv')
                c.dma(s, w8o[tok0:tok0 + 128, :], outt[:, 0:8], 'wst')

    @blk.sync
    def _(s):
        c.wait(s, 'st')
        c.wait(s, 'lst')
        c.wait(s, 'wst')

    return c.close()


def run_l2c(inputs, x1full, attns):
    if 'l2c' not in _CACHE:
        _CACHE['l2c'] = build_l2c()
    nc = _CACHE['l2c']
    base = {
        "projT": np.ascontiguousarray(inputs['L0_ffn_proj_weight'].T),
        "ln2wb": np.ascontiguousarray(np.repeat(inputs['L0_ffn_ln2_weight'][None, :], 128, 0)),
        "ln2bb": np.ascontiguousarray(np.repeat(inputs['L0_ffn_ln2_bias'][None, :], 128, 0)),
        "rw": np.ascontiguousarray(inputs['L0_ffn_router_weight']),
        "ident": np.eye(128, dtype=np.float32),
    }
    in_maps = []
    for cc in range(NCORES):
        m = dict(base)
        m["x1"] = np.ascontiguousarray(x1full[cc * ROWS:(cc + 1) * ROWS])
        st = np.stack([np.asarray(attns[k][cc * ROWS:(cc + 1) * ROWS] if False else
                                  attns[k][:, :, 4 * cc:4 * cc + 4]) for k in range(NCORES)])
        m["attnfm"] = np.ascontiguousarray(st.transpose(1, 3, 0, 2).reshape(D, ROWS, T))
        in_maps.append(m)
    res = run_bass_kernel_spmd(nc, in_maps, core_ids=list(range(NCORES)))
    return ([res.results[cc]["x2p"] for cc in range(NCORES)],
            [res.results[cc]["ln2o"] for cc in range(NCORES)],
            [res.results[cc]["w8o"] for cc in range(NCORES)])


def build_l2d():
    c = Ctx()
    nc = c.nc
    TOK = ROWS * T
    ln2 = nc.dram_tensor("ln2", [TOK, D], BF16, kind="ExternalInput").ap()
    w8 = nc.dram_tensor("w8", [TOK, 8], F32, kind="ExternalInput").ap()
    x2p = nc.dram_tensor("x2p", [TOK, D], F32, kind="ExternalInput").ap()
    mw1 = nc.dram_tensor("mw1", [8, D, 4096], F32, kind="ExternalInput").ap()
    mw2 = nc.dram_tensor("mw2", [8, 4096, D], F32, kind="ExternalInput").ap()
    x2 = nc.dram_tensor("x2", [TOK, D], F32, kind="ExternalOutput").ap()

    xt = c.sb("xt", [128, 8, 512], BF16)
    w1b = c.sb("w1b", [128, 8, 4096], BF16)
    w2b = c.sb("w2b", [128, 32, 1024], BF16)
    gbuf = c.sb("gbuf", [128, 32, 512], BF16)
    w8b = c.sb("w8b", [128, 64, 8], F32)
    xacc = c.sb("xacc", [128, 1024], F32)
    xout = c.sb("xout", [128, 1024], F32)

    p = [c.ps(f"p{i}") for i in range(6)]

    blk = c.start_block()

    @blk.gpsimd
    def _(g):
        c.dma(g, w8b[:], w8.rearrange("(a p) e -> p a e", p=128), 'gld')

    for e in range(8):
        @blk.gpsimd
        def _(g, e=e):
            c.wait(g, 'mm', c.cnt.get('wfree', 0))
            for dc in range(8):
                c.dma(g, w1b[:, dc, :], mw1[e, dc * 128:(dc + 1) * 128, :], 'ewl')
            for fc in range(32):
                c.dma(g, w2b[:, fc, :], mw2[e, fc * 128:(fc + 1) * 128, :], 'ewl')

        for tc in range(16):
            @blk.sync
            def _(s, e=e, tc=tc):
                c.wait(s, 'mm', c.cnt.get('xtmm', 0))
                for dc in range(8):
                    c.dma(s, xt[:, dc, :],
                          ln2[tc * 512:(tc + 1) * 512, dc * 128:(dc + 1) * 128],
                          'xtl', transpose=True)

            for fc in range(32):
                bank = fc % 2

                @blk.tensor
                def _(te, e=e, tc=tc, fc=fc, bank=bank):
                    if fc == 0:
                        c.wait(te, 'xtl')
                        if tc == 0:
                            c.wait(te, 'ewl')
                    c.wait(te, 'ds', c.cnt.get('gef%d' % bank, 0))
                    for dc in range(8):
                        ins = te.matmul(p[bank][:], w1b[:, dc, fc * 128:(fc + 1) * 128],
                                        xt[:, dc, :], start=(dc == 0), stop=(dc == 7))
                    c.inc(ins, 'mm', 1)
                    c.cnt['f1mm%d' % bank] = c.cnt['mm']
                    if fc == 31:
                        c.cnt['xtmm'] = c.cnt['mm']

                @blk.scalar
                def _(sc, fc=fc, bank=bank):
                    c.wait(sc, 'mm', c.cnt['f1mm%d' % bank])
                    sc.activation(gbuf[:, fc, :], p[bank][:], AF.Gelu)
                    c.cnt['gef%d' % bank] = c.cnt['ds']

            for tk4 in range(4):
                tkk = tc * 4 + tk4

                @blk.sync
                def _(s, e=e, tkk=tkk):
                    c.wait(s, 'dv')
                    src = x2p if e == 0 else x2
                    c.wait(s, 'x2s%d' % (tkk % 4), c.cnt.get('x2st%d' % (tkk % 4), 0))
                    c.dma(s, xacc[:], src[tkk * 128:(tkk + 1) * 128, :], 'xal')

                @blk.tensor
                def _(te, tc=tc, tk4=tk4):
                    if tk4 == 0:
                        c.wait(te, 'ds')
                    for hh in range(2):
                        bank = 2 + hh
                        c.wait(te, 'dv', c.cnt.get('f2f%d' % bank, 0))
                        for fc in range(32):
                            ins = te.matmul(p[bank][:], gbuf[:, fc, tk4 * 128:(tk4 + 1) * 128],
                                            w2b[:, fc, hh * 512:(hh + 1) * 512],
                                            start=(fc == 0), stop=(fc == 31))
                        c.inc(ins, 'mm', 1)
                        c.cnt['f2mm%d' % bank] = c.cnt['mm']
                        if tk4 == 3 and hh == 1:
                            c.cnt['wfree'] = c.cnt['mm']

                @blk.vector
                def _(v, e=e, tc=tc, tk4=tk4, tkk=tkk):
                    c.wait(v, 'xal')
                    for hh in range(2):
                        bank = 2 + hh
                        c.wait(v, 'mm', c.cnt['f2mm%d' % bank])
                        hsl = slice(hh * 512, (hh + 1) * 512)
                        v.scalar_tensor_tensor(xout[:, hsl], p[bank][:],
                                               w8b[:, tkk, e:e + 1], xacc[:, hsl],
                                               OP.mult, OP.add)
                        c.cnt['f2f%d' % bank] = c.cnt['dv']

                @blk.sync
                def _(s, tkk=tkk):
                    c.wait(s, 'dv')
                    c.dma(s, x2[tkk * 128:(tkk + 1) * 128, :], xout[:], 'x2s%d' % (tkk % 4))
                    c.cnt['x2st%d' % (tkk % 4)] = c.cnt['x2s%d' % (tkk % 4)]

    @blk.sync
    def _(s):
        for i in range(4):
            c.wait(s, 'x2s%d' % i)

    return c.close()


def run_l2d(inputs, ln2s, w8s, x2ps):
    if 'l2d' not in _CACHE:
        _CACHE['l2d'] = build_l2d()
    nc = _CACHE['l2d']
    base = {
        "mw1": np.ascontiguousarray(inputs['L0_ffn_moe_w1']),
        "mw2": np.ascontiguousarray(inputs['L0_ffn_moe_w2']),
    }
    in_maps = []
    for cc in range(NCORES):
        m = dict(base)
        m["ln2"] = np.ascontiguousarray(ln2s[cc])
        m["w8"] = np.ascontiguousarray(w8s[cc])
        m["x2p"] = np.ascontiguousarray(x2ps[cc].reshape(ROWS * T, D))
        in_maps.append(m)
    res = run_bass_kernel_spmd(nc, in_maps, core_ids=list(range(NCORES)))
    return np.concatenate([res.results[cc]["x2"].reshape(ROWS, T, D) for cc in range(NCORES)], axis=0)
